# version 02
# baseline (speedup 1.0000x reference)
import numpy as np
import ml_dtypes
import concourse.bass as bass
import concourse.mybir as mybir
from concourse.bass_utils import run_bass_kernel_spmd
from contextlib import ExitStack

F32 = mybir.dt.float32
BF16 = mybir.dt.bfloat16
AF = mybir.ActivationFunctionType
ALU = mybir.AluOpType
AX = mybir.AxisListType

NCORE = 8
NB = 4
T = 2048
D = 1024
DC = D // 128
NH = 16
NHP = 8
E = 8
FF = 2048
FC = FF // 128
CH = 4
TT = 512
NIT = NB * CH
NT = NB * T
NG = NT // 128
SINK_ITERS = 3

DBG_PHASES = 3
DBG_DUMP = False
DBG_NO_INTER = False


def build_core_program():
    nc = bass.Bass("TRN2", target_bir_lowering=False, detect_race_conditions=False)

    x_fm = nc.dram_tensor("x_fm", [D, NT], F32, kind="ExternalInput")
    wqkvT = nc.dram_tensor("wqkvT", [D, 3 * D], F32, kind="ExternalInput")
    bqkv_col = nc.dram_tensor("bqkv_col", [128, 24], F32, kind="ExternalInput")
    bqkv_rowv = nc.dram_tensor("bqkv_rowv", [1, D], F32, kind="ExternalInput")
    woutT_d = nc.dram_tensor("woutT", [D, D], F32, kind="ExternalInput")
    bout_col = nc.dram_tensor("bout_col", [128, 8], F32, kind="ExternalInput")
    routerT = nc.dram_tensor("routerT", [D, 8], F32, kind="ExternalInput")
    rb_col = nc.dram_tensor("rb_col", [8, 1], F32, kind="ExternalInput")
    rb_row_d = nc.dram_tensor("rb_row", [1, 8], F32, kind="ExternalInput")
    tempv = nc.dram_tensor("tempv", [1, 1], F32, kind="ExternalInput")
    tempv128 = nc.dram_tensor("tempv128", [128, 1], F32, kind="ExternalInput")
    quadn = nc.dram_tensor("quadn", [2, 1], F32, kind="ExternalInput")
    quadw = nc.dram_tensor("quadw", [2, 1], F32, kind="ExternalInput")
    wblk_d = nc.dram_tensor("wblk", [NHP, 128, 128], F32, kind="ExternalInput")
    wg_d = nc.dram_tensor("wg", [E, D, FF], F32, kind="ExternalInput")
    wp_d = nc.dram_tensor("wp", [E, D, FF], F32, kind="ExternalInput")
    wo_d = nc.dram_tensor("wo", [E, FF, D], F32, kind="ExternalInput")
    out_fm = nc.dram_tensor("out_fm", [D, NT], F32, kind="ExternalOutput")
    xnscr = nc.dram_tensor("xnscr", [D, NT], BF16, kind="Internal")
    if DBG_DUMP:
        dbg_qf = nc.dram_tensor("dbg_qf", [128, NHP * TT], BF16, kind="ExternalOutput")
        dbg_kf = nc.dram_tensor("dbg_kf", [128, NHP * TT], BF16, kind="ExternalOutput")
        dbg_km = nc.dram_tensor("dbg_km", [128, NHP * TT], BF16, kind="ExternalOutput")
        dbg_v = nc.dram_tensor("dbg_v", [128, CH * D], BF16, kind="ExternalOutput")
        dbg_at = nc.dram_tensor("dbg_at", [128, DC * TT], BF16, kind="ExternalOutput")
        dbg_bc = nc.dram_tensor("dbg_bc", [64, TT], F32, kind="ExternalOutput")
        dbg_dr = nc.dram_tensor("dbg_dr", [1, TT], BF16, kind="ExternalOutput")
        dbg_sm = nc.dram_tensor("dbg_sm", [128, 2 * TT], BF16, kind="ExternalOutput")
        dbg_kv = nc.dram_tensor("dbg_kv", [128, NHP * 64], F32, kind="ExternalOutput")
        dbg_sr = nc.dram_tensor("dbg_sr", [8, TT], F32, kind="ExternalOutput")
        dbg_bs = nc.dram_tensor("dbg_bs", [128, NG * 8], F32, kind="ExternalOutput")
        dbg_ks = nc.dram_tensor("dbg_ks", [128, NHP], F32, kind="ExternalOutput")

    es = ExitStack()
    with es:
        sb = lambda n, s, d: es.enter_context(nc.sbuf_tensor(n, s, d))
        sem = lambda n: es.enter_context(nc.semaphore(n))
        P = [es.enter_context(nc.psum_tensor(f"P{i}", [128, 512], F32)) for i in range(8)]

        ones_col = sb("ones_col", [128, 1], BF16)
        ones_row = sb("ones_row", [1, 128], BF16)
        ones2 = sb("ones2", [128, 2], BF16)
        pat2 = sb("pat2", [2, 128], BF16)
        patRf = sb("patRf", [2, 128], F32)
        ident = sb("ident", [128, 128], F32)
        ident_bf = sb("ident_bf", [128, 128], BF16)
        maskx = sb("maskx", [128, 896], BF16)
        onesrf = sb("onesrf", [1, 512], F32)
        selone = sb("selone", [8, 8 * 128], BF16)
        rtemp1 = sb("rtemp1", [1, 1], F32)
        rt128 = sb("rt128", [128, 1], F32)
        rb_sb = sb("rb_sb", [8, 1], F32)
        rbr = sb("rbr", [1, 8], F32)
        qv3 = sb("qv3", [2, 3], F32)
        svec = sb("svec", [128, 3], F32)
        Bsc = sb("Bsc", [128, NG * 8], F32)

        s_prep = sem("s_prep")
        s_prep2 = sem("s_prep2")
        s_gp = sem("s_gp")
        s_gv = sem("s_gv")
        s_ga = sem("s_ga")
        s_pm = sem("s_pm")
        s_pe = sem("s_pe")

        s_endA = [sem(f"s_endA{i}") for i in range(5)]

        esA = ExitStack()
        sbA = lambda n, s, d: esA.enter_context(nc.sbuf_tensor(n, s, d))
        with esA:
            wqkv_bf = sbA("wqkv_bf", [128, DC * 3 * D], BF16)
            wout_bf = sbA("wout_bf", [128, DC * D], BF16)
            router_sb = sbA("router_sb", [128, DC * 8], F32)
            bqkv_sb = sbA("bqkv_sb", [128, 24], F32)
            bout_sb = sbA("bout_sb", [128, 8], F32)
            bqr_hi = sbA("bqr_hi", [1, D], BF16)
            bqr_lo = sbA("bqr_lo", [1, D], BF16)
            wblk_bf = sbA("wblk_bf", [128, NHP * 128], BF16)
            xt = sbA("xt", [128, DC * TT], F32)
            wstg = [xt[:, 0:2048], xt[:, 2048:4096]]
            sqt = [sbA(f"sqt{i}", [128, TT], BF16) for i in range(2)]
            ln_tmp = sbA("ln_tmp", [2, TT], F32)
            mrow = sbA("mrow", [1, TT], F32)
            rrow = sbA("rrow", [1, TT], F32)
            mh = sbA("mh", [1, TT], BF16)
            ml = sbA("ml", [1, TT], BF16)
            rh = sbA("rh", [1, TT], BF16)
            rl = sbA("rl", [1, TT], BF16)
            xn_fp = sbA("xn_fp", [128, DC * TT], F32)
            bqr_f = xn_fp[0:1, 0:D]
            xnb = sbA("xnb", [128, DC * TT], BF16)
            xbf = xnb
            srow = sbA("srow", [8, TT], F32)
            qk_t = [sbA(f"qk_t{i}", [128, TT], BF16) for i in range(2)]
            sqq = [sbA(f"sqq{i}", [128, TT], BF16) for i in range(2)]
            qn_t = [sbA(f"qn_t{i}", [128, TT], BF16) for i in range(2)]
            nr_row = sbA("nr_row", [2, TT], F32)
            nr_hi = sbA("nr_hi", [2, TT], BF16)
            nr_lo = sbA("nr_lo", [2, TT], BF16)
            ft_t = sbA("ft_t", [128, TT], F32)
            ft_e = sbA("ft_e", [128, TT], BF16)
            ft_p = sbA("ft_p", [128, TT], BF16)
            qf_sb = sbA("qf_sb", [128, NHP * TT], BF16)
            kf_sb = sbA("kf_sb", [128, NHP * TT], BF16)
            kftm_sb = sbA("kftm_sb", [128, NHP * TT], BF16)
            v_sb = sbA("v_sb", [128, CH * D], BF16)
            Sm_all = sbA("Sm_all", [128, 1280], BF16)
            SMOF = [0, 512, 896, 1152]
            attn_sb = sbA("attn_sb", [128, DC * TT], BF16)
            dr_hi = sbA("dr_hi", [1, TT], BF16)
            dr_lo = sbA("dr_lo", [1, TT], BF16)
            bc_sb = sbA("bc_sb", [64, TT], F32)
            den_rr = sbA("den_rr", [1, TT], F32)
            den_i = ln_tmp[0:1, :]
            den_r = den_rr[0:1, :]
            ctx_f = ft_t[0:64, :]
            KV_sb = sbA("KV_sb", [128, NHP * 65], F32)
            KV_bf = sbA("KV_bf", [128, NHP * 65], BF16)
            out_t = [sbA(f"out_t{i}", [128, TT], F32) for i in range(2)]

            sA_wl = [sem("sA_wl0"), sem("sA_wl1")]
            sA_wc = sem("sA_wc")
            sA_x = [sem("sA_x0"), sem("sA_x1")]
            sA_xc = sem("sA_xc")
            sA_sq = sem("sA_sq")
            sA_st = sem("sA_st")
            sA_mr = sem("sA_mr")
            sA_vr = sem("sA_vr")
            sA_ar = sem("sA_ar")
            sA_bc = sem("sA_bc")
            sA_xn = sem("sA_xn")
            sA_xb = sem("sA_xb")
            sA_xs = sem("sA_xs")
            sA_sc = sem("sA_sc")
            sA_se = sem("sA_se")
            sA_sv = sem("sA_sv")
            sA_sp = sem("sA_sp")
            sA_sb = sem("sA_sb")
            sA_qm = sem("sA_qm")
            sA_qe = sem("sA_qe")
            sA_qb = sem("sA_qb")
            sA_sr = sem("sA_sr")
            sA_qs = sem("sA_qs")
            sA_nm = sem("sA_nm")
            sA_ns = sem("sA_ns")
            sA_nr = sem("sA_nr")
            sA_nb = sem("sA_nb")
            sA_qn = sem("sA_qn")
            sA_pj = sem("sA_pj")
            sA_fv = sem("sA_fv")
            sA_fa = sem("sA_fa")
            sA_ff = sem("sA_ff")
            sA_km = sem("sA_km")
            sA_kf = sem("sA_kf")
            sA_vm = sem("sA_vm")
            sA_ve = sem("sA_ve")
            sA_sm = sem("sA_sm")
            sA_sd = sem("sA_sd")
            sA_pb = sem("sA_pb")
            sA_dd = sem("sA_dd")
            sA_bm = sem("sA_bm")
            sA_be = sem("sA_be")
            sA_at = sem("sA_at")
            sA_om = sem("sA_om")
            sA_oe = sem("sA_oe")
            sA_o1 = sem("sA_o1")
            sA_o2 = sem("sA_o2")
            sA_ao = sem("sA_ao")

            block = es.enter_context(nc.Block())

            NWA = 3 * DC + NHP

            @block.sync
            def _(sync):
                sync.dma_start(bqkv_sb[:, :], bqkv_col[:, :]).then_inc(s_prep, 16)
                sync.dma_start(bout_sb[:, :], bout_col[:, :]).then_inc(s_prep, 16)
                sync.dma_start(xn_fp[0:1, 0:D], bqkv_rowv[:, :]).then_inc(s_prep, 16)
                sync.dma_start(router_sb.rearrange("p (c e) -> p c e", e=8),
                               routerT.rearrange("(c p) e -> p c e", p=128)).then_inc(s_prep, 16)
                sync.dma_start(qv3[0:2, 0:1], quadn[:, :]).then_inc(s_prep, 16)
                sync.dma_start(qv3[0:2, 2:3], quadw[:, :]).then_inc(s_prep, 16)
                sync.dma_start(rt128[:, :], tempv128[:, :]).then_inc(s_prep, 16)
                sync.dma_start(rb_sb[:, :], rb_col[:, :]).then_inc(s_prep, 16)
                sync.dma_start(rbr[:, :], rb_row_d[:, :]).then_inc(s_prep2, 16)
                wl = 0
                def stage(width, src_ap):
                    nonlocal wl
                    sync.wait_ge(sA_wc, max(0, wl - 1))
                    sync.dma_start(wstg[wl % 2][:, 0:width], src_ap).then_inc(sA_wl[wl % 2], 16)
                    wl += 1
                for dc in range(DC):
                    stage(1536, wqkvT.rearrange("(c p) e -> p c e", p=128)[:, dc, 0:1536])
                    stage(1536, wqkvT.rearrange("(c p) e -> p c e", p=128)[:, dc, 1536:3072])
                for dc in range(DC):
                    stage(D, woutT_d.rearrange("(c p) e -> p c e", p=128)[:, dc, :])
                for hp in range(NHP):
                    stage(128, wblk_d[hp, :, :])
                sync.wait_ge(sA_wc, NWA)
                for it in range(NIT):
                    if it >= 1:
                        sync.wait_ge(sA_o1, it * DC)
                    sync.dma_start(
                        xt.rearrange("p (c t) -> p c t", c=DC),
                        x_fm.rearrange("(c p) t -> p c t", p=128)[:, :, it * TT:(it + 1) * TT],
                    ).then_inc(sA_x[0], 16)

            @block.gpsimd
            def _(gpsimd):
                g = 0
                def mark(op):
                    nonlocal g
                    g += 1
                    op.then_inc(s_gp, 1)
                    gpsimd.wait_ge(s_gp, g)
                mark(gpsimd.memset(ones_col[:, :], 1.0))
                mark(gpsimd.memset(ones_row[:, :], 1.0))
                mark(gpsimd.memset(ones2[:, :], 1.0))
                mark(gpsimd.affine_select(ones2[:, :], ones2[:, :], [[-64, 2]],
                                          ALU.is_ge, 0.0, base=0, channel_multiplier=1))
                mark(gpsimd.affine_select(ones2[:, :], ones2[:, :], [[64, 2]],
                                          ALU.is_ge, 0.0, base=63, channel_multiplier=-1))
                mark(gpsimd.memset(pat2[:, :], 1.0))
                mark(gpsimd.affine_select(pat2[:, :], pat2[:, :], [[1, 128]],
                                          ALU.is_ge, 0.0, base=0, channel_multiplier=-64))
                mark(gpsimd.affine_select(pat2[:, :], pat2[:, :], [[-1, 128]],
                                          ALU.is_ge, 0.0, base=63, channel_multiplier=64))
                mark(gpsimd.memset(patRf[:, :], 1.0))
                mark(gpsimd.affine_select(patRf.rearrange("r (a b c) -> r a b c", b=2, c=32),
                                          patRf.rearrange("r (a b c) -> r a b c", b=2, c=32),
                                          [[0, 2], [1, 2], [0, 32]], ALU.is_equal, 0.0,
                                          base=0, channel_multiplier=-1))

                mark(gpsimd.memset(ident[:, :], 1.0))
                mark(gpsimd.affine_select(ident[:, :], ident[:, :], [[-1, 128]],
                                          ALU.is_equal, 0.0, base=0, channel_multiplier=1))
                mark(gpsimd.tensor_copy(ident_bf[:, :], ident[:, :]))
                mark(gpsimd.memset(maskx[:, :], 1.0))
                mark(gpsimd.affine_select(maskx[:, :], maskx[:, :], [[1, 896]], ALU.is_ge, 0.0,
                                          base=-384, channel_multiplier=-1))
                mark(gpsimd.memset(onesrf[:, :], 1.0))
                mark(gpsimd.memset(selone[:, :], 1.0))
                mark(gpsimd.affine_select(selone.rearrange("p (a q) -> p a q", a=8),
                                          selone.rearrange("p (a q) -> p a q", a=8),
                                          [[-1, 8], [0, 128]], ALU.is_equal, 0.0,
                                          base=0, channel_multiplier=1))
                for it in range(NIT):
                    for dc in range(DC):
                        gpsimd.wait_ge(sA_xn, it * DC + dc + 1)
                        if dc == 0 and it > 0:
                            gpsimd.wait_ge(sA_vm, it * 8)
                        gpsimd.tensor_copy(xnb[:, dc * TT:(dc + 1) * TT],
                                           xn_fp[:, dc * TT:(dc + 1) * TT]).then_inc(sA_xb, 1)
                    for oc in range(16):
                        gpsimd.wait_ge(sA_qe, it * 16 + oc + 1)
                        gpsimd.tensor_scalar_add(qk_t[oc % 2][:, :], qk_t[oc % 2][:, :],
                                                 bqkv_sb[:, oc:oc + 1]).then_inc(sA_qb, 1)
                    for oc in range(DC):
                        gpsimd.wait_ge(sA_o1, it * DC + oc + 1)
                        gpsimd.tensor_tensor(out_t[oc % 2][:, :],
                                             out_t[oc % 2][:, :],
                                             xn_fp[:, oc * TT:(oc + 1) * TT],
                                             ALU.add).then_inc(sA_o2, 1)
                        gpsimd.dma_start(
                            out_fm.rearrange("(c p) t -> p c t", p=128)[:, oc, it * TT:(it + 1) * TT],
                            out_t[oc % 2][:, :],
                        ).then_inc(sA_ao, 16)
                if DBG_DUMP:
                    gpsimd.wait_ge(sA_at, NIT * NH)
                    gpsimd.dma_start(dbg_qf[:, :], qf_sb[:, :]).then_inc(sA_ao, 16)
                    gpsimd.dma_start(dbg_kf[:, :], kf_sb[:, :]).then_inc(sA_ao, 16)
                    gpsimd.dma_start(dbg_km[:, :], kftm_sb[:, :]).then_inc(sA_ao, 16)
                    gpsimd.dma_start(dbg_v[:, :], v_sb[:, :]).then_inc(sA_ao, 16)
                    gpsimd.dma_start(dbg_at[:, :], attn_sb[:, :]).then_inc(sA_ao, 16)
                    gpsimd.dma_start(dbg_bc[:, :], bc_sb[:, :]).then_inc(sA_ao, 16)
                    gpsimd.dma_start(dbg_dr[:, :], dr_hi[:, :]).then_inc(sA_ao, 16)
                    gpsimd.dma_start(dbg_sm[:, 0:1024], Sm_all[:, 0:1024]).then_inc(sA_ao, 16)
                    gpsimd.dma_start(dbg_kv[:, :], KV_sb[:, 0:NHP * 64]).then_inc(sA_ao, 16)
                    gpsimd.dma_start(dbg_ks[:, :], KV_sb[:, 64:NHP * 65:65], allow_slow_non_contiguous=True).then_inc(sA_ao, 16)
                    gpsimd.dma_start(dbg_sr[:, :], srow[:, :]).then_inc(sA_ao, 16)
                    gpsimd.dma_start(dbg_bs[:, :], Bsc[:, :]).then_inc(sA_ao, 16)
                    gpsimd.wait_ge(sA_ao, 16 * NIT + 192)
                gpsimd.wait_ge(sA_ao, 16 * NIT).then_inc(s_endA[3], 1)

            @block.scalar
            def _(scalar):
                wl = 0
                for dc in range(DC):
                    for hf in range(2):
                        scalar.wait_ge(sA_wl[wl % 2], 16 * (wl // 2 + 1))
                        scalar.copy(wqkv_bf[:, dc * 3 * D + hf * 1536: dc * 3 * D + (hf + 1) * 1536],
                                    wstg[wl % 2][:, 0:1536]).then_inc(sA_wc, 1)
                        wl += 1
                for dc in range(DC):
                    scalar.wait_ge(sA_wl[wl % 2], 16 * (wl // 2 + 1))
                    scalar.copy(wout_bf[:, dc * D:(dc + 1) * D], wstg[wl % 2][:, 0:D]).then_inc(sA_wc, 1)
                    wl += 1
                for hp in range(NHP):
                    scalar.wait_ge(sA_wl[wl % 2], 16 * (wl // 2 + 1))
                    scalar.copy(wblk_bf[:, hp * 128:(hp + 1) * 128], wstg[wl % 2][:, 0:128]).then_inc(sA_wc, 1)
                    wl += 1
                scalar.wait_ge(s_prep, 16 * 8)
                scalar.copy(bqr_hi[:, :], bqr_f).then_inc(s_ga, 1)
                scalar.wait_ge(s_gv, 3)
                scalar.sqrt(qv3[0:2, 1:2], qv3[0:2, 1:2])
                scalar.sqrt(qv3[0:2, 2:3], qv3[0:2, 2:3]).then_inc(s_ga, 1)

                for it in range(NIT):
                    scalar.wait_ge(sA_x[0], 16 * (it + 1))
                    if it > 0:
                        scalar.wait_ge(sA_st, it * DC)
                        scalar.wait_ge(sA_vm, it * 8)
                    scalar.copy(xbf[:, :], xt[:, :]).then_inc(sA_xc, 1)
                    for dc in range(DC):
                        if it * DC + dc >= 2:
                            scalar.wait_ge(sA_st, it * DC + dc - 1)
                        scalar.activation(sqt[dc % 2][:, :], xt[:, dc * TT:(dc + 1) * TT],
                                          AF.Square).then_inc(sA_sq, 1)
                    scalar.wait_ge(sA_vr, it + 1)
                    scalar.activation(rrow[:, :], rrow[:, :], AF.Sqrt).then_inc(sA_ar, 1)
                    scalar.wait_ge(sA_sc, it + 1)
                    if it > 0:
                        scalar.wait_ge(sA_sp, it)
                    scalar.activation(srow[:, :], P[6][0:8, :], AF.Copy).then_inc(sA_se, 1)
                    for oc in range(16):
                        idx = it * 16 + oc
                        scalar.wait_ge(sA_qm, idx + 1)
                        if idx >= 2:
                            scalar.wait_ge(sA_qn, idx - 1)
                            scalar.wait_ge(sA_nm, idx - 1)
                        scalar.activation(qk_t[oc % 2][:, :], P[oc % 2][:, :],
                                          AF.Copy).then_inc(sA_qe, 1)
                        scalar.wait_ge(sA_qb, it * 16 + oc + 1)
                        scalar.activation(sqq[oc % 2][:, :], qk_t[oc % 2][:, :],
                                          AF.Square).then_inc(sA_qs, 1)
                        scalar.wait_ge(sA_nm, it * 16 + oc + 1)
                        scalar.activation(nr_row[:, :], P[2][0:2, :], AF.Sqrt).then_inc(sA_ns, 1)
                        scalar.wait_ge(sA_fv, it * 16 + oc + 1)
                        scalar.activation(ft_e[:, :], ft_t[:, :], AF.Exp)
                        scalar.activation(ft_p[:, :], P[5][:, :], AF.Square).then_inc(sA_fa, 1)
                    for vg in range(8):
                        scalar.wait_ge(sA_vm, it * 8 + vg + 1)
                        if it > 0:
                            scalar.wait_ge(sA_pb, it * NH)
                        tc, vb = vg // 2, vg % 2
                        scalar.activation(v_sb[:, tc * D + vb * 512: tc * D + (vb + 1) * 512],
                                          P[vg % 2][:, :], AF.Copy).then_inc(sA_ve, 1)
                    for h in range(NH):
                        hidx = it * NH + h
                        scalar.wait_ge(sA_bm, hidx + 1)
                        scalar.activation(bc_sb[:, :], P[7][0:64, :], AF.Copy).then_inc(sA_be, 1)
                    for oc in range(DC):
                        scalar.wait_ge(sA_om, it * DC + oc + 1)
                        if it * DC + oc >= 2:
                            scalar.wait_ge(sA_ao, 16 * (it * DC + oc - 1))
                        scalar.activation(out_t[oc % 2][:, :], P[oc % 2][:, :],
                                          AF.Copy).then_inc(sA_oe, 1)
                    scalar.wait_ge(sA_xb, (it + 1) * DC)
                    if it > 0:
                        scalar.wait_ge(sA_xs, 16 * it)
                    scalar.dma_start(
                        xnscr.rearrange("(c p) t -> p c t", p=128)[:, :, it * TT:(it + 1) * TT],
                        xnb.rearrange("p (c t) -> p c t", c=DC),
                    ).then_inc(sA_xs, 16)
                scalar.wait_ge(sA_xs, 16 * NIT).then_inc(s_endA[2], 1)

            @block.vector
            def _(vector):
                vector.wait_ge(s_ga, 1)
                vector.tensor_sub(bqr_lo[:, :], bqr_f, bqr_hi[:, :]).then_inc(s_gv, 1)
                vector.wait_ge(s_prep, 16 * 8)
                vector.tensor_scalar_max(rt128[:, :], rt128[:, :], 0.1)
                vector.reciprocal(rt128[:, :], rt128[:, :]).then_inc(s_gv, 1)
                vector.tensor_scalar(qv3[0:2, 1:2], qv3[0:2, 0:1], 0.0, 2.0, ALU.max, ALU.mult)
                vector.tensor_scalar_max(qv3[0:2, 2:3], qv3[0:2, 2:3], 0.0).then_inc(s_gv, 1)
                vector.wait_ge(s_ga, 2)
                vector.tensor_scalar_mul(qv3[0:2, 2:3], qv3[0:2, 2:3], 1.0 / 32.0).then_inc(s_gv, 1)
                vector.wait_ge(s_pm, 1)
                vector.wait_ge(s_prep, 16 * 8)
                vector.tensor_copy(svec[:, :], P[5][:, 0:3])
                vector.tensor_scalar_mul(router_sb[:, :], router_sb[:, :], rt128[:, 0:1])
                vector.tensor_scalar_mul(rb_sb[:, :], rb_sb[:, :], rt128[0:8, 0:1])
                vector.wait_ge(s_prep2, 16)
                vector.tensor_scalar_mul(rbr[:, :], rbr[:, :], rt128[0:1, 0:1]).then_inc(s_pe, 1)
                for it in range(NIT):
                    vector.wait_ge(sA_st, (it + 1) * DC)
                    vector.tensor_scalar_mul(mrow[:, :], P[2][0:1, :], 1.0 / D)
                    vector.tensor_scalar_mul(rrow[:, :], P[3][0:1, :], 1.0 / D)
                    vector.tensor_tensor(ln_tmp[0:1, :], mrow[:, :], mrow[:, :], ALU.mult)
                    vector.tensor_sub(rrow[:, :], rrow[:, :], ln_tmp[0:1, :])
                    vector.tensor_scalar_add(rrow[:, :], rrow[:, :], 1e-5).then_inc(sA_vr, 1)
                    vector.wait_ge(sA_ar, it + 1)
                    vector.reciprocal(rrow[:, :], rrow[:, :])
                    vector.tensor_copy(mh[:, :], mrow[:, :])
                    vector.tensor_sub(ln_tmp[0:1, :], mrow[:, :], mh[:, :])
                    vector.tensor_copy(ml[:, :], ln_tmp[0:1, :])
                    vector.tensor_copy(rh[:, :], rrow[:, :])
                    vector.tensor_sub(ln_tmp[0:1, :], rrow[:, :], rh[:, :])
                    vector.tensor_copy(rl[:, :], ln_tmp[0:1, :]).then_inc(sA_mr, 1)
                    vector.wait_ge(sA_bc, it + 1)
                    if it > 0:
                        vector.wait_ge(sA_o2, it * DC)
                    for dc in range(DC):
                        vector.tensor_tensor(xn_fp[:, dc * TT:(dc + 1) * TT],
                                             xt[:, dc * TT:(dc + 1) * TT], P[4][:, :], ALU.subtract)
                        vector.tensor_tensor(xn_fp[:, dc * TT:(dc + 1) * TT],
                                             xn_fp[:, dc * TT:(dc + 1) * TT], P[5][:, :],
                                             ALU.mult).then_inc(sA_xn, 1)
                    vector.wait_ge(sA_sp, it + 1)
                    vector.tensor_copy(Bsc[:, it * 32:(it + 1) * 32], P[7][:, 0:32]).then_inc(sA_sb, 1)
                    for oc in range(16):
                        vector.wait_ge(sA_ns, it * 16 + oc + 1)
                        vector.tensor_scalar_max(nr_row[:, :], nr_row[:, :], 1e-12)
                        vector.reciprocal(nr_row[:, :], nr_row[:, :])
                        vector.tensor_copy(nr_hi[:, :], nr_row[:, :])
                        vector.tensor_sub(ln_tmp[0:2, :], nr_row[:, :], nr_hi[:, :])
                        vector.tensor_copy(nr_lo[:, :], ln_tmp[0:2, :]).then_inc(sA_nr, 1)
                        vector.wait_ge(sA_nb, it * 16 + oc + 1)
                        vector.wait_ge(sA_qb, it * 16 + oc + 1)
                        if it * 16 + oc >= 2:
                            vector.wait_ge(sA_pj, it * 16 + oc - 1)
                        vector.tensor_tensor(qn_t[oc % 2][:, :], qk_t[oc % 2][:, :], P[4][:, :],
                                             ALU.mult).then_inc(sA_qn, 1)
                        vector.wait_ge(sA_pj, it * 16 + oc + 1)
                        vector.tensor_scalar(ft_t[:, :], P[5][:, :], svec[:, 1:2], svec[:, 0:1],
                                             ALU.mult, ALU.subtract)
                        vector.tensor_scalar(ft_t[:, :], ft_t[:, :], 20.0, -20.0,
                                             ALU.min, ALU.max).then_inc(sA_fv, 1)
                        vector.wait_ge(sA_fa, it * 16 + oc + 1)
                        if it > 0 and oc == 0:
                            vector.wait_ge(sA_pb, it * NH)
                        dst = qf_sb if oc < 8 else kf_sb
                        hp = oc % 8
                        vector.scalar_tensor_tensor(dst[:, hp * TT:(hp + 1) * TT],
                                                    ft_e[:, :], svec[:, 2:3], ft_p[:, :],
                                                    ALU.mult, ALU.mult).then_inc(sA_ff, 1)
                        if oc >= 8:
                            kc = oc - 8
                            vector.wait_ge(sA_km, it * NHP + kc + 1)
                            if it > 0 and kc == 0:
                                vector.wait_ge(sA_pb, it * NH)
                            vector.tensor_copy(kftm_sb[:, kc * TT:(kc + 1) * TT],
                                               P[3].bitcast(BF16)[:, 0:TT]).then_inc(sA_kf, 1)
                    for h in range(NH):
                        hidx = it * NH + h
                        hh, hp = h % 2, h // 2
                        C = it % CH
                        fb = 64 * hh
                        for sg in range(CH):
                            vector.wait_ge(sA_sm, hidx * CH + sg + 1)
                            vector.tensor_tensor(Sm_all[:, SMOF[sg]:SMOF[sg] + TT - 128 * sg],
                                                 P[2 + sg % 2][:, sg * 128:TT],
                                                 maskx[:, 384:896 - 128 * sg],
                                                 ALU.mult).then_inc(sA_sd, 1)
                        vector.wait_ge(sA_pb, hidx + 1)
                        if C > 0 and not DBG_NO_INTER:
                            vector.tensor_copy(den_i, P[5][64:65, :])
                            vector.tensor_tensor(den_r, P[6][0:1, :], den_i, ALU.add)
                        else:
                            vector.tensor_copy(den_r, P[6][0:1, :])
                        vector.tensor_scalar_max(den_r, den_r, 1e-6)
                        vector.reciprocal(den_r, den_r)
                        vector.tensor_copy(dr_hi[:, :], den_r)
                        vector.tensor_sub(den_i, den_r, dr_hi[:, :])
                        vector.tensor_copy(dr_lo[:, :], den_i).then_inc(sA_dd, 1)
                        vector.wait_ge(sA_be, hidx + 1)
                        if it > 0 and h == 0:
                            vector.wait_ge(sA_om, it * DC)
                        if C > 0 and not DBG_NO_INTER:
                            vector.tensor_copy(ctx_f, P[5][0:64, :])
                            vector.tensor_tensor(ctx_f, P[4][0:64, :], ctx_f, ALU.add)
                            vector.tensor_tensor(attn_sb[fb:fb + 64, hp * TT:(hp + 1) * TT],
                                                 ctx_f, bc_sb[:, :], ALU.mult)
                        else:
                            vector.tensor_tensor(attn_sb[fb:fb + 64, hp * TT:(hp + 1) * TT],
                                                 P[4][0:64, :], bc_sb[:, :], ALU.mult)
                        kvs = KV_sb[fb:fb + 64, hp * 65: hp * 65 + 65]
                        kvb = KV_bf[fb:fb + 64, hp * 65: hp * 65 + 65]
                        if C > 0:
                            vector.tensor_tensor(kvs, kvs, P[0][fb:fb + 64, 0:65], ALU.add)
                        else:
                            vector.tensor_copy(kvs, P[0][fb:fb + 64, 0:65])
                        vector.tensor_copy(kvb, kvs).then_inc(sA_at, 1)
                    for oc in range(DC):
                        vector.wait_ge(sA_oe, it * DC + oc + 1)
                        vector.scalar_tensor_tensor(out_t[oc % 2][:, :],
                                                    out_t[oc % 2][:, :],
                                                    bout_sb[:, oc:oc + 1],
                                                    xt[:, oc * TT:(oc + 1) * TT],
                                                    ALU.add, ALU.add).then_inc(sA_o1, 1)
                vector.sem_inc(s_endA[1], 1)

            @block.tensor
            def _(tensor):
                tensor.wait_ge(s_gp, 18)
                tensor.wait_ge(s_gv, 4)
                tensor.wait_ge(s_prep, 16 * 8)
                tensor.matmul(P[5][:, 0:3], patRf[:, :], qv3[:, :], start=True, stop=True).then_inc(s_pm, 1)
                tensor.wait_ge(sA_wc, NWA)
                tensor.wait_ge(s_pe, 1)
                for it in range(NIT):
                    sl = it % 2
                    C = it % CH
                    tensor.wait_ge(sA_xc, it + 1)
                    if it > 0:
                        tensor.wait_ge(sA_mr, it)
                        tensor.wait_ge(sA_sd, it * NH * CH)
                    for dc in range(DC):
                        tensor.matmul(P[2][0:1, :], ones_col[:, :], xbf[:, dc * TT:(dc + 1) * TT],
                                      start=(dc == 0), stop=(dc == DC - 1))
                        tensor.wait_ge(sA_sq, it * DC + dc + 1)
                        tensor.matmul(P[3][0:1, :], ones_col[:, :], sqt[dc % 2][:, :],
                                      start=(dc == 0), stop=(dc == DC - 1)).then_inc(sA_st, 1)
                    tensor.wait_ge(sA_mr, it + 1)
                    if it > 0:
                        tensor.wait_ge(sA_xn, it * DC)
                        tensor.wait_ge(sA_at, it * NH)
                    tensor.matmul(P[4][:, :], ones_row[:, :], mh[:, :], start=True, stop=False)
                    tensor.matmul(P[4][:, :], ones_row[:, :], ml[:, :], start=False, stop=True)
                    tensor.matmul(P[5][:, :], ones_row[:, :], rh[:, :], start=True, stop=False)
                    tensor.matmul(P[5][:, :], ones_row[:, :], rl[:, :], start=False, stop=True).then_inc(sA_bc, 1)
                    if it > 0:
                        tensor.wait_ge(sA_se, it)
                        tensor.wait_ge(sA_dd, it * NH)
                    for dc in range(DC):
                        tensor.wait_ge(sA_xn, it * DC + dc + 1)
                        tensor.matmul(P[6][0:8, :], router_sb[:, dc * 8:(dc + 1) * 8],
                                      xn_fp[:, dc * TT:(dc + 1) * TT],
                                      start=(dc == 0), stop=False)
                    tensor.matmul(P[6][0:8, :], rbr[:, :], onesrf[:, :],
                                  start=False, stop=True).then_inc(sA_sc, 1)
                    tensor.wait_ge(sA_se, it + 1)
                    if it > 0:
                        tensor.wait_ge(sA_sb, it)
                        tensor.wait_ge(sA_be, it * NH)
                    for tc in range(4):
                        op = tensor.transpose(P[7][:, tc * 8:(tc + 1) * 8],
                                              srow[0:8, tc * 128:(tc + 1) * 128], ident[0:8, 0:8])
                    op.then_inc(sA_sp, 1)
                    tensor.wait_ge(sA_xb, (it + 1) * DC)
                    if it > 0:
                        tensor.wait_ge(sA_at, it * NH)
                    for oc in range(16):
                        if oc == 0 and it > 0:
                            tensor.wait_ge(sA_oe, it * DC)
                        if it * 16 + oc >= 2:
                            tensor.wait_ge(sA_qe, it * 16 + oc - 1)
                        for dc in range(DC):
                            op = tensor.matmul(P[oc % 2][:, :],
                                               wqkv_bf[:, dc * 3 * D + oc * 128: dc * 3 * D + (oc + 1) * 128],
                                               xnb[:, dc * TT:(dc + 1) * TT],
                                               start=(dc == 0), stop=(dc == DC - 1))
                        op.then_inc(sA_qm, 1)
                        tensor.wait_ge(sA_qs, it * 16 + oc + 1)
                        if it * 16 + oc > 0:
                            tensor.wait_ge(sA_ns, it * 16 + oc)
                        tensor.matmul(P[2][0:2, :], ones2[:, :], sqq[oc % 2][:, :],
                                      start=True, stop=True).then_inc(sA_nm, 1)
                        tensor.wait_ge(sA_nr, it * 16 + oc + 1)
                        if it * 16 + oc > 0:
                            tensor.wait_ge(sA_qn, it * 16 + oc)
                        tensor.matmul(P[4][:, :], pat2[:, :], nr_hi[:, :], start=True, stop=False)
                        tensor.matmul(P[4][:, :], pat2[:, :], nr_lo[:, :], start=False, stop=True).then_inc(sA_nb, 1)
                        tensor.wait_ge(sA_qn, it * 16 + oc + 1)
                        if it * 16 + oc > 0:
                            tensor.wait_ge(sA_fv, it * 16 + oc)
                            tensor.wait_ge(sA_fa, it * 16 + oc)
                        hp = oc % 8
                        tensor.matmul(P[5][:, :], wblk_bf[:, hp * 128:(hp + 1) * 128],
                                      qn_t[oc % 2][:, :], start=True, stop=True).then_inc(sA_pj, 1)
                        if oc >= 8:
                            kc = oc - 8
                            tensor.wait_ge(sA_ff, it * 16 + oc + 1)
                            if it * NHP + kc > 0:
                                tensor.wait_ge(sA_kf, it * NHP + kc)
                            for tc in range(4):
                                op = tensor.transpose(P[3].bitcast(BF16)[:, tc * 128:(tc + 1) * 128],
                                                      kf_sb[:, hp * TT + tc * 128: hp * TT + (tc + 1) * 128],
                                                      ident_bf[:, :])
                            op.then_inc(sA_km, 1)
                    for vg in range(8):
                        tc, vb = vg // 2, vg % 2
                        if vg == 0:
                            tensor.wait_ge(sA_qe, (it + 1) * 16)
                        if it * 8 + vg >= 2:
                            tensor.wait_ge(sA_ve, it * 8 + vg - 1)
                        for dc in range(DC):
                            tensor.matmul(P[vg % 2][:, :],
                                          xnb[:, dc * TT + tc * 128: dc * TT + (tc + 1) * 128],
                                          wqkv_bf[:, dc * 3 * D + 2 * D + vb * 512: dc * 3 * D + 2 * D + (vb + 1) * 512],
                                          start=(dc == 0), stop=False)
                        tensor.matmul(P[vg % 2][:, :], ones_row[:, :],
                                      bqr_hi[:, vb * 512:(vb + 1) * 512], start=False, stop=False)
                        tensor.matmul(P[vg % 2][:, :], ones_row[:, :],
                                      bqr_lo[:, vb * 512:(vb + 1) * 512], start=False, stop=True).then_inc(sA_vm, 1)
                    for h in range(NH):
                        hidx = it * NH + h
                        hh, hp = h % 2, h // 2
                        fb = 64 * hh
                        if h == 0:
                            tensor.wait_ge(sA_ff, it * 16 + 16)
                            tensor.wait_ge(sA_kf, (it + 1) * NHP)
                            tensor.wait_ge(sA_ve, (it + 1) * 8)
                            tensor.wait_ge(sA_ns, (it + 1) * 16)
                            tensor.wait_ge(sA_se, it + 1)
                        for sg in range(CH):
                            if hidx * CH + sg >= 2:
                                tensor.wait_ge(sA_sd, hidx * CH + sg - 1)
                            tensor.matmul(P[2 + sg % 2][:, sg * 128:TT],
                                          kf_sb[fb:fb + 64, hp * TT + sg * 128: hp * TT + (sg + 1) * 128],
                                          qf_sb[fb:fb + 64, hp * TT + sg * 128: hp * TT + TT],
                                          start=True, stop=True).then_inc(sA_sm, 1)
                        if hidx > 0:
                            tensor.wait_ge(sA_dd, hidx)
                            tensor.wait_ge(sA_at, hidx)
                        for sg in range(CH):
                            tensor.wait_ge(sA_sd, hidx * CH + sg + 1)
                            tensor.matmul(P[6][0:1, sg * 128:TT], ones_col[:, :],
                                          Sm_all[:, SMOF[sg]:SMOF[sg] + TT - 128 * sg],
                                          start=(sg == 0), stop=(sg == CH - 1))
                        for sg in range(CH):
                            tensor.matmul(P[4][0:64, sg * 128:TT],
                                          v_sb[:, sg * D + h * 64: sg * D + h * 64 + 64],
                                          Sm_all[:, SMOF[sg]:SMOF[sg] + TT - 128 * sg],
                                          start=(sg == 0), stop=(sg == CH - 1))
                        if C > 0 and not DBG_NO_INTER:
                            tensor.matmul(P[5][0:65, :], KV_bf[fb:fb + 64, hp * 65: hp * 65 + 65],
                                          qf_sb[fb:fb + 64, hp * TT:(hp + 1) * TT],
                                          start=True, stop=True)
                        for sg in range(CH):
                            tensor.matmul(P[0][fb:fb + 64, 0:64],
                                          kftm_sb[:, hp * TT + sg * 128 + fb: hp * TT + sg * 128 + fb + 64],
                                          v_sb[:, sg * D + h * 64: sg * D + h * 64 + 64],
                                          start=(sg == 0), stop=(sg == CH - 1))
                        for sg in range(CH):
                            op = tensor.matmul(P[0][fb:fb + 64, 64:65],
                                               kftm_sb[:, hp * TT + sg * 128 + fb: hp * TT + sg * 128 + fb + 64],
                                               ones_col[:, :],
                                               start=(sg == 0), stop=(sg == CH - 1))
                        op.then_inc(sA_pb, 1)
                        tensor.wait_ge(sA_dd, hidx + 1)
                        if h == 0:
                            tensor.wait_ge(sA_sb, it + 1)
                        if hidx > 0:
                            tensor.wait_ge(sA_be, hidx)
                        tensor.matmul(P[7][0:64, :], ones_row[:, 0:64], dr_hi[:, :], start=True, stop=False)
                        tensor.matmul(P[7][0:64, :], ones_row[:, 0:64], dr_lo[:, :],
                                      start=False, stop=True).then_inc(sA_bm, 1)
                    tensor.wait_ge(sA_at, (it + 1) * NH)
                    for oc in range(DC):
                        if it * DC + oc >= 2:
                            tensor.wait_ge(sA_oe, it * DC + oc - 1)
                        for dc in range(DC):
                            op = tensor.matmul(P[oc % 2][:, :],
                                               wout_bf[:, dc * D + oc * 128: dc * D + (oc + 1) * 128],
                                               attn_sb[:, dc * TT:(dc + 1) * TT],
                                               start=(dc == 0), stop=(dc == DC - 1))
                        op.then_inc(sA_om, 1)
                tensor.sem_inc(s_endA[0], 1)

        if DBG_PHASES >= 2:
            esF = ExitStack()
            sbF = lambda n, s, d: esF.enter_context(nc.sbuf_tensor(n, s, d))
            with esF:
                W_hi = sbF("W_hi", [8, NT], BF16)
                W_lo = sbF("W_lo", [8, NT], BF16)
                wgF = sbF("wgF", [128, DC * FF], BF16)
                wpF = sbF("wpF", [128, DC * FF], BF16)
                woF = sbF("woF", [128, FC * D], BF16)
                wstF = [sbF(f"wstF{i}", [128, 2048], F32) for i in range(2)]
                xnF = sbF("xnF", [128, DC * TT], BF16)
                hF = sbF("hF", [128, FC * TT], BF16)
                gF = sbF("gF", [128, TT], BF16)
                outF = sbF("outF", [128, DC * TT], F32)
                bcF = sbF("bcF", [128, TT], F32)
                t1f = sbF("t1f", [128, NG * 8], F32)
                tb1 = sbF("tb1", [128, NG * 8], BF16)
                maxo = sbF("maxo", [128, NG * 8], F32)
                me_ = sbF("me_", [128, NG], F32)
                se_ = sbF("se_", [128, NG], F32)
                mb_ = sbF("mb_", [128, 32], F32)
                gr_ = sbF("gr_", [128, NG], F32)
                mm32 = sbF("mm32", [32, 1], F32)
                mrowS = sbF("mrowS", [1, 32], F32)
                mrow_bf = sbF("mrow_bf", [1, 32], BF16)
                lrow = sbF("lrow", [1, 32], F32)
                lrow_h = sbF("lrow_h", [1, 32], BF16)
                lrow_l = sbF("lrow_l", [1, 32], BF16)
                s32 = sbF("s32", [1, 32], F32)

                S0cp = sbF("S0cp", [128, NG * 8], F32)
                sS = sem("sS")
                sS_W = sem("sS_W")
                sF_wl = [sem("sF_wl0"), sem("sF_wl1")]
                sF_wc = sem("sF_wc")
                sF_xl = sem("sF_xl")
                sF_gm = sem("sF_gm")
                sF_ge = sem("sF_ge")
                sF_hm = sem("sF_hm")
                sF_om = sem("sF_om")
                sF_oe = sem("sF_oe")
                sF_bc = sem("sF_bc")
                sF_st = sem("sF_st")

                BARS = [(sA_om, NIT * DC), (sA_o1, NIT * DC), (sA_o2, NIT * DC),
                        (sA_oe, NIT * DC), (sA_xs, 16 * NIT), (sA_x[0], 16 * NIT),
                        (sA_qb, NIT * 16), (sA_at, NIT * NH), (sA_kf, NIT * NHP),
                        (sA_ao, 16 * NIT + (80 if DBG_DUMP else 0))]

                def bar(eng):
                    for s_, v_ in BARS:
                        eng.wait_ge(s_, v_)

                r3 = lambda ap: ap.rearrange("p (g e) -> p g e", e=8)
                r4 = lambda ap: ap.rearrange("p (b c e) -> p b e c", b=NB, c=16)
                bc3 = lambda ap: ap.rearrange("p g -> p g 1" if False else "p (g o) -> p g o", o=1).broadcast_to([128, NG, 8])
                bc4 = lambda ap: ap.rearrange("p (b e o) -> p b e o", b=NB, o=1).broadcast_to([128, NB, 8, 16])

                steps = []
                def st(en, fn):
                    steps.append((en, fn))
                st("g", lambda g: g.tensor_copy(S0cp[:, :], Bsc[:, :]))
                for i in range(SINK_ITERS):
                    st("v", lambda v: v.reduce_max(me_[:, :], r3(Bsc), AX.X))
                    st("v", lambda v: v.tensor_tensor(r3(t1f), r3(Bsc), bc3(me_), ALU.subtract))
                    st("a", lambda a: a.activation(t1f[:, :], t1f[:, :], AF.Exp))
                    st("v", lambda v: v.reduce_sum(se_[:, :], r3(t1f), AX.X))
                    st("a", lambda a: a.activation(se_[:, :], se_[:, :], AF.Ln))
                    st("v", lambda v: v.tensor_tensor(se_[:, :], se_[:, :], me_[:, :], ALU.add))
                    st("v", lambda v: v.tensor_tensor(r3(Bsc), r3(Bsc), bc3(se_), ALU.subtract))
                    st("v", lambda v: v.reduce_max(mb_[:, 0:32], r4(Bsc), AX.X))
                    st("t", lambda t: t.transpose(P[2][0:32, 0:128], mb_[:, 0:32], ident[:, :]))
                    st("v", lambda v: v.reduce_max(mm32[:, :], P[2][0:32, 0:128], AX.X))
                    st("t", lambda t: t.transpose(P[3][0:1, 0:32], mm32[:, :], ident[0:32, 0:32]))
                    st("v", lambda v: v.tensor_copy(mrowS[:, :], P[3][0:1, 0:32]))
                    st("v", lambda v: v.tensor_copy(mrow_bf[:, :], mrowS[:, :]))
                    st("t", lambda t: t.matmul(P[4][:, 0:32], ones_row[:, :], mrow_bf[:, :],
                                               start=True, stop=True))
                    st("v", lambda v: v.tensor_copy(mb_[:, :], P[4][:, 0:32]))
                    st("v", lambda v: v.tensor_tensor(r4(t1f), r4(Bsc), bc4(mb_), ALU.subtract))
                    st("a", lambda a: a.activation(t1f[:, :], t1f[:, :], AF.Exp))
                    st("g", lambda g: g.tensor_copy(tb1[:, :], t1f[:, :]))
                    st("t", lambda t: t.matmul(P[5][0:1, :], ones_col[:, :], tb1[:, :],
                                               start=True, stop=True))
                    st("v", lambda v: v.reduce_sum(
                        s32.rearrange("o (b e) -> o b e", e=8),
                        P[5][0:1, :].rearrange("o (b c e) -> o b e c", b=NB, c=16), AX.X))
                    st("a", lambda a: a.activation(s32[:, :], s32[:, :], AF.Ln))
                    st("v", lambda v: v.tensor_tensor(lrow[:, :], s32[:, :], mrow_bf[:, :], ALU.add))
                    st("v", lambda v: v.tensor_copy(lrow_h[:, :], lrow[:, :]))
                    st("v", lambda v: v.tensor_sub(lrow[:, :], lrow[:, :], lrow_h[:, :]))
                    st("v", lambda v: v.tensor_copy(lrow_l[:, :], lrow[:, :]))
                    st("t", lambda t: t.matmul(P[4][:, 0:32], ones_row[:, :], lrow_h[:, :],
                                               start=True, stop=False))
                    st("t", lambda t: t.matmul(P[4][:, 0:32], ones_row[:, :], lrow_l[:, :],
                                               start=False, stop=True))
                    st("v", lambda v: v.tensor_copy(mb_[:, :], P[4][:, 0:32]))
                    st("v", lambda v: v.tensor_tensor(r4(Bsc), r4(Bsc), bc4(mb_), ALU.subtract))
                st("a", lambda a: a.activation(Bsc[:, :], Bsc[:, :], AF.Exp))
                st("v", lambda v: v.reduce_sum(gr_[:, :], r3(Bsc), AX.X))
                st("v", lambda v: v.tensor_scalar_add(gr_[:, :], gr_[:, :], 1e-8))
                st("v", lambda v: v.reciprocal(gr_[:, :], gr_[:, :]))
                st("v", lambda v: v.tensor_tensor(r3(Bsc), r3(Bsc), bc3(gr_), ALU.mult))
                for g64 in range(NG):
                    st("v", (lambda gg: (lambda v: v.max(out=maxo[:, gg * 8:(gg + 1) * 8],
                                                         in_=Bsc[:, gg * 8:(gg + 1) * 8])))(g64))
                m1v = maxo.rearrange("p (g e) -> p g e", e=8)[:, :, 0:1]
                m2v = maxo.rearrange("p (g e) -> p g e", e=8)[:, :, 1:2]
                st("v", lambda v: v.tensor_tensor(gr_.rearrange("p g -> p g").unsqueeze(2) if False else gr_[:, :].rearrange("p (g o) -> p g o", o=1), m1v, m2v, ALU.add))
                st("v", lambda v: v.tensor_scalar_add(gr_[:, :], gr_[:, :], 1e-8))
                st("v", lambda v: v.reciprocal(gr_[:, :], gr_[:, :]))
                st("v", lambda v: v.tensor_tensor(r3(t1f), r3(Bsc), m2v.broadcast_to([128, NG, 8]), ALU.is_ge))
                st("v", lambda v: v.tensor_tensor(t1f[:, :], t1f[:, :], Bsc[:, :], ALU.mult))
                st("v", lambda v: v.tensor_tensor(r3(t1f), r3(t1f), bc3(gr_), ALU.mult))
                for g64 in range(NG):
                    st("t", (lambda gg: (lambda t: t.transpose(
                        P[2 + gg % 2][0:8, 0:128], t1f[:, gg * 8:(gg + 1) * 8], ident[:, :])))(g64))
                    st("v", (lambda gg: (lambda v: v.tensor_copy(
                        W_hi[:, gg * 128:(gg + 1) * 128], P[2 + gg % 2][0:8, 0:128])))(g64))
                    st("v", (lambda gg: (lambda v: v.tensor_tensor(
                        W_lo[:, gg * 128:(gg + 1) * 128], P[2 + gg % 2][0:8, 0:128],
                        W_hi[:, gg * 128:(gg + 1) * 128], ALU.subtract)))(g64))
                NSTEP = len(steps)

                def emit_seq(eng, name):
                    for idx, (en, fn) in enumerate(steps):
                        if en == name:
                            eng.wait_ge(sS, idx)
                            fn(eng).then_inc(sS, 1)

                @block.sync
                def _(sync):
                    bar(sync)
                    wl = 0
                    def stageF(e_, kind, c):
                        nonlocal wl
                        sync.wait_ge(sF_wc, max(0, wl - 1))
                        srct = {"g": wg_d, "p": wp_d, "o": wo_d}[kind]
                        width = 2048 if kind != "o" else 1024
                        sync.dma_start(wstF[wl % 2][:, 0:width],
                                       srct[e_, c * 128:(c + 1) * 128, :]).then_inc(sF_wl[wl % 2], 16)
                        wl += 1
                    def load_weights(e_):
                        for c in range(DC):
                            stageF(e_, "g", c)
                        for c in range(DC):
                            stageF(e_, "p", c)
                        for c in range(FC):
                            stageF(e_, "o", c)
                    load_weights(0)
                    for e_ in range(E):
                        for tt in range(NIT):
                            it = e_ * NIT + tt
                            if it > 0:
                                sync.wait_ge(sF_gm, it * 2 * FC)
                            sync.dma_start(
                                xnF.rearrange("p (c t) -> p c t", c=DC),
                                xnscr.rearrange("(c p) t -> p c t", p=128)[:, :, tt * TT:(tt + 1) * TT],
                            ).then_inc(sF_xl, 16)
                        if e_ + 1 < E:
                            load_weights(e_ + 1)

                @block.scalar
                def _(scalar):
                    bar(scalar)
                    emit_seq(scalar, "a")
                    wl = 0
                    for e_ in range(E):
                        scalar.wait_ge(sF_gm, e_ * NIT * 2 * FC)
                        scalar.wait_ge(sF_om, e_ * NIT * DC)
                        for kind, dst, nchunk in (("g", wgF, DC), ("p", wpF, DC), ("o", woF, FC)):
                            for c in range(nchunk):
                                scalar.wait_ge(sF_wl[wl % 2], 16 * (wl // 2 + 1))
                                if kind == "o":
                                    scalar.copy(dst[:, c * D:(c + 1) * D], wstF[wl % 2][:, 0:D]).then_inc(sF_wc, 1)
                                else:
                                    scalar.copy(dst[:, c * FF:(c + 1) * FF], wstF[wl % 2][:, :]).then_inc(sF_wc, 1)
                                wl += 1
                        for tt in range(NIT):
                            it = e_ * NIT + tt
                            for fc in range(FC):
                                scalar.wait_ge(sF_gm, it * 2 * FC + 2 * fc + 1)
                                scalar.wait_ge(sF_hm, it * FC + fc)
                                scalar.activation(gF[:, :], P[0][:, :], AF.Relu).then_inc(sF_ge, 1)

                if DBG_DUMP:
                    dbg_W = nc.dram_tensor("dbg_W", [8, 2 * NT], BF16, kind="ExternalOutput")
                    dbg_G = nc.dram_tensor("dbg_G", [128, NG * 8], F32, kind="ExternalOutput")
                    dbg_S0 = nc.dram_tensor("dbg_S0", [128, NG * 8], F32, kind="ExternalOutput")
                @block.gpsimd
                def _(gpsimd):
                    bar(gpsimd)
                    emit_seq(gpsimd, "g")
                    if DBG_DUMP:
                        gpsimd.wait_ge(sS, NSTEP)
                        gpsimd.dma_start(dbg_W[:, 0:NT], W_hi[:, :]).then_inc(sF_st, 16)
                        gpsimd.dma_start(dbg_W[:, NT:2 * NT], W_lo[:, :]).then_inc(sF_st, 16)
                        gpsimd.dma_start(dbg_G[:, :], Bsc[:, :]).then_inc(sF_st, 16)
                        gpsimd.dma_start(dbg_S0[:, :], S0cp[:, :]).then_inc(sF_st, 16)
                        gpsimd.wait_ge(sF_st, 64)
                    for e_ in range(E):
                        for tt in range(NIT):
                            it = e_ * NIT + tt
                            gpsimd.wait_ge(sF_oe, (it + 1) * DC)
                            gpsimd.dma_start(
                                out_fm.rearrange("(c p) t -> p c t", p=128)[:, :, tt * TT:(tt + 1) * TT],
                                outF.rearrange("p (c t) -> p c t", c=DC),
                                accum_op=ALU.add,
                            ).then_inc(sF_st, 16)
                    gpsimd.wait_ge(sF_st, 16 * E * NIT + (64 if DBG_DUMP else 0))

                @block.vector
                def _(vector):
                    bar(vector)
                    emit_seq(vector, "v")
                    for e_ in range(E):
                        for tt in range(NIT):
                            it = e_ * NIT + tt
                            vector.wait_ge(sF_bc, 2 * it + 1)
                            vector.tensor_copy(bcF[:, :], P[1][:, :]).then_inc(sF_bc, 1)
                            for fc in range(FC):
                                vector.wait_ge(sF_ge, it * FC + fc + 1)
                                vector.wait_ge(sF_gm, it * 2 * FC + 2 * fc + 2)
                                vector.tensor_tensor(hF[:, fc * TT:(fc + 1) * TT],
                                                     gF[:, :], P[7][:, :], ALU.mult).then_inc(sF_hm, 1)
                            vector.wait_ge(sF_st, 16 * it)
                            for dc in range(DC):
                                vector.wait_ge(sF_om, it * DC + dc + 1)
                                vector.tensor_tensor(outF[:, dc * TT:(dc + 1) * TT],
                                                     P[2 + dc % 2][:, :], bcF[:, :],
                                                     ALU.mult).then_inc(sF_oe, 1)

                @block.tensor
                def _(tensor):
                    bar(tensor)
                    emit_seq(tensor, "t")
                    for e_ in range(E):
                        tensor.wait_ge(sF_wc, (e_ + 1) * (2 * DC + FC))
                        for tt in range(NIT):
                            it = e_ * NIT + tt
                            t0 = tt * TT
                            tensor.wait_ge(sF_bc, 2 * it)
                            if it == 0:
                                tensor.wait_ge(sS, NSTEP)
                            tensor.matmul(P[1][:, :], selone[:, e_ * 128:(e_ + 1) * 128],
                                          W_hi[:, t0:t0 + TT], start=True, stop=False)
                            tensor.matmul(P[1][:, :], selone[:, e_ * 128:(e_ + 1) * 128],
                                          W_lo[:, t0:t0 + TT], start=False, stop=True).then_inc(sF_bc, 1)
                            tensor.wait_ge(sF_xl, 16 * (it + 1))
                            for fc in range(FC):
                                tensor.wait_ge(sF_ge, it * FC + fc)
                                for dc in range(DC):
                                    op = tensor.matmul(P[0][:, :],
                                                       wgF[:, dc * FF + fc * 128: dc * FF + (fc + 1) * 128],
                                                       xnF[:, dc * TT:(dc + 1) * TT],
                                                       start=(dc == 0), stop=(dc == DC - 1))
                                op.then_inc(sF_gm, 1)
                                tensor.wait_ge(sF_hm, it * FC + fc)
                                for dc in range(DC):
                                    op = tensor.matmul(P[7][:, :],
                                                       wpF[:, dc * FF + fc * 128: dc * FF + (fc + 1) * 128],
                                                       xnF[:, dc * TT:(dc + 1) * TT],
                                                       start=(dc == 0), stop=(dc == DC - 1))
                                op.then_inc(sF_gm, 1)
                            for dc in range(DC):
                                if it * DC + dc >= 2:
                                    tensor.wait_ge(sF_oe, it * DC + dc - 1)
                                for fc in range(FC):
                                    if dc == 0:
                                        tensor.wait_ge(sF_hm, it * FC + fc + 1)
                                    op = tensor.matmul(P[2 + dc % 2][:, :],
                                                       woF[:, fc * D + dc * 128: fc * D + (dc + 1) * 128],
                                                       hF[:, fc * TT:(fc + 1) * TT],
                                                       start=(fc == 0), stop=(fc == FC - 1))
                                op.then_inc(sF_om, 1)

    return nc


def prep_inputs(inputs, core):
    x = inputs["L0_attn_x"][NB * core: NB * (core + 1)]
    x_fm = np.ascontiguousarray(x.transpose(2, 0, 1).reshape(D, NT), dtype=np.float32)
    d = {
        "x_fm": x_fm,
        "wqkvT": np.ascontiguousarray(inputs["L0_attn_w_qkv"].T, dtype=np.float32),
        "bqkv_col": np.ascontiguousarray(inputs["L0_attn_b_qkv"].reshape(24, 128).T, dtype=np.float32),
        "bqkv_rowv": np.ascontiguousarray(inputs["L0_attn_b_qkv"][2 * D:].reshape(1, D), dtype=np.float32),
        "woutT": np.ascontiguousarray(inputs["L0_attn_w_out"].T, dtype=np.float32),
        "bout_col": np.ascontiguousarray(inputs["L0_attn_b_out"].reshape(8, 128).T, dtype=np.float32),
        "routerT": np.ascontiguousarray(inputs["L0_ffn_router_w"].T, dtype=np.float32),
        "rb_col": np.ascontiguousarray(inputs["L0_ffn_router_b"].reshape(8, 1), dtype=np.float32),
        "rb_row": np.ascontiguousarray(inputs["L0_ffn_router_b"].reshape(1, 8), dtype=np.float32),
        "tempv": np.ascontiguousarray(inputs["L0_ffn_temperature"].reshape(1, 1), dtype=np.float32),
        "tempv128": np.ascontiguousarray(np.repeat(inputs["L0_ffn_temperature"].reshape(1, 1), 128, axis=0), dtype=np.float32),
        "quadn": np.ascontiguousarray(inputs["L0_attn_quad_nodes"].reshape(2, 1), dtype=np.float32),
        "quadw": np.ascontiguousarray(inputs["L0_attn_quad_weights"].reshape(2, 1), dtype=np.float32),
        "wg": np.ascontiguousarray(inputs["L0_ffn_wg"], dtype=np.float32),
        "wp": np.ascontiguousarray(inputs["L0_ffn_wp"], dtype=np.float32),
        "wo": np.ascontiguousarray(inputs["L0_ffn_wo"], dtype=np.float32),
    }
    om = np.asarray(inputs["L0_attn_omega"], dtype=np.float32)
    wblk = np.zeros((NHP, 128, 128), np.float32)
    for hp in range(NHP):
        for hh in range(2):
            h = 2 * hp + hh
            for r in range(2):
                wblk[hp, 64 * hh:64 * hh + 64, 64 * hh + 32 * r: 64 * hh + 32 * r + 32] = om[r, h]
    d["wblk"] = wblk
    return d


_NC_CACHE = {}


def kernel(**inputs):
    if "prog" not in _NC_CACHE:
        _NC_CACHE["prog"] = build_core_program()
    nc = _NC_CACHE["prog"]
    in_maps = []
    shared = None
    for core in range(NCORE):
        d = prep_inputs(inputs, core)
        if shared is None:
            shared = d
        else:
            for k in list(d.keys()):
                if k != "x_fm":
                    d[k] = shared[k]
        in_maps.append(d)
    res = run_bass_kernel_spmd(nc, in_maps, core_ids=list(range(NCORE)))
    outs = []
    for core in range(NCORE):
        o = res.results[core]["out_fm"]
        outs.append(o.reshape(D, NB, T).transpose(1, 2, 0))
    return np.ascontiguousarray(np.concatenate(outs, axis=0), dtype=np.float32)
